# version 03 (final)
# baseline (speedup 1.0000x reference)
import numpy as np
from contextlib import ExitStack

import concourse.bass as bass
import concourse.mybir as mybir
from concourse.bass_utils import run_bass_kernel_spmd

F32 = mybir.dt.float32
BF16 = mybir.dt.bfloat16
AF = mybir.ActivationFunctionType
ALU = mybir.AluOpType
AX = mybir.AxisListType.X

D, T, NDT = 1024, 4096, 8
N0 = 64
E0, H0 = 8, 2048
NH, DH = 16, 64
E1, C1, I1 = 32, 1024, 512
EPS = 1e-5
NEG = -1.0e30


def build_program():
    nc = bass.Bass("TRN2", target_bir_lowering=False)
    es = ExitStack()
    cnt = {}

    def _inc(n, k):
        cnt[n] = cnt.get(n, 0) + k
        return cnt[n]

    def cg(n):
        return cnt.get(n, 0)

    def din(name, shape, dt):
        return nc.dram_tensor(name, shape, dt, kind="ExternalInput").ap()

    xin = din("xin", [NDT, 128, T], F32)
    wg_i = din("wg_i", [128, 4 * NDT * N0], F32)
    lnw_i = din("lnw_i", [128, NDT], F32)
    lnb_i = din("lnb_i", [128, NDT], F32)
    bp_i = din("bp_i", [128, NDT], F32)
    b4_i = din("b4_i", [N0, 4], F32)
    wp_i = din("wp_i", [N0, D], F32)
    w12_i = din("w12_i", [E0, NDT, 128, 2 * H0], F32)
    w3_i = din("w3_i", [E0, 16, 128, D], F32)
    gw0_i = din("gw0_i", [128, NDT * E0], F32)
    q_i = din("q_i", [NH, NDT, 128, DH], F32)
    k_i = din("k_i", [NH, NDT, 128, DH], F32)
    v_i = din("v_i", [NH, NDT, 128, DH], F32)
    wh_i = din("wh_i", [NH, DH, D], F32)
    gw1_i = din("gw1_i", [128, NDT * E1], BF16)
    biasbc_i = din("biasbc_i", [128, E1], F32)
    w1_i = din("w1_i", [E1, NDT, 128, I1], BF16)
    w3e_i = din("w3e_i", [E1, NDT, 128, I1], BF16)
    w2_i = din("w2_i", [E1, 4, 128, D], BF16)
    sw1_i = din("sw1_i", [NDT, 128, I1], BF16)
    sw3_i = din("sw3_i", [NDT, 128, I1], BF16)
    sw2_i = din("sw2_i", [4, 128, D], BF16)
    identb_i = din("identb_i", [128, 128], BF16)
    identf_i = din("identf_i", [128, 128], F32)
    cmask_i = din("cmask_i", [4, 128, 512], F32)
    basemask_i = din("basemask_i", [E1, 8], F32)
    ones1_i = din("ones1_i", [1, 128], BF16)

    y = nc.dram_tensor("y", [NDT, 128, T], F32, kind="ExternalOutput").ap()
    xd1 = nc.dram_tensor("xd1", [NDT, 128, T], F32).ap()
    xd2 = nc.dram_tensor("xd2", [NDT, 128, T], F32).ap()
    cc_in = nc.dram_tensor("cc_in", [E1, 1], F32)
    cc_out = nc.dram_tensor("cc_out", [8, E1, 1], F32)

    def sb(stack, name, shape, dt):
        return stack.enter_context(nc.sbuf_tensor(name, shape, dt))

    P8 = es.enter_context(nc.psum_tensor("P8", [128, 4096], F32))
    sems = {}
    for s in ("gd", "tPE", "tDVE", "tACT", "ccs"):
        sems[s] = es.enter_context(nc.semaphore(s))
    block = es.enter_context(nc.Block())
    TICKS = ("gd", "tPE", "tDVE", "tACT", "ccs")

    def tick(inst, s, n=1):
        inst.then_inc(sems[s], n)
        _inc(s, n)

    def chain(steps):
        for name, fn in steps:
            pre = {s: cg(s) for s in TICKS}

            def _emit(e, fn=fn, pre=pre, name=name):
                for s, v in pre.items():
                    if v:
                        e.wait_ge(sems[s], v)
                inst = fn(e)
                tsem = {"vector": "tDVE", "scalar": "tACT",
                        "tensor": "tPE", "gpsimd": "tPOOL"}[name]
                tick(inst, tsem)
            getattr(block, name)(_emit)

    def gdma(pairs):
        pre = {s: cg(s) for s in TICKS}

        @block.gpsimd
        def _(g):
            for s, v in pre.items():
                if v:
                    g.wait_ge(sems[s], v)
            for dst, src in pairs:
                tick(g.dma_start(out=dst, in_=src), "gd", 16)

    def coll():
        pre = {s: cg(s) for s in TICKS}

        @block.gpsimd
        def _(g):
            for s, v in pre.items():
                if v:
                    g.wait_ge(sems[s], v)
            tick(g.collective_compute(
                "AllGather", ALU.bypass,
                replica_groups=[[0, 1, 2, 3, 4, 5, 6, 7]],
                ins=[cc_in.ap()[:, :]], outs=[cc_out.ap()[:, :, :]]), "ccs")

    def join_all():
        marks = {s: cg(s) for s in TICKS}
        for name in ("sync", "tensor", "vector", "scalar", "gpsimd"):
            def _j(e, marks=marks):
                for s, v in marks.items():
                    if v:
                        e.wait_ge(sems[s], v)
            getattr(block, name)(_j)

    def mm(t, out, pairs):
        n = len(pairs)
        r = None
        for idx, (w_, x_) in enumerate(pairs):
            r = t.matmul(out, w_, x_, start=(idx == 0), stop=(idx == n - 1))
        return r

    identb = sb(es, "identb", [128, 128], BF16)
    identf = sb(es, "identf", [128, 128], F32)
    ones1b = sb(es, "ones1b", [1, 128], BF16)
    onesqb = sb(es, "onesqb", [128, 128], BF16)
    gdma([(identb[:, :], identb_i), (identf[:, :], identf_i),
          (ones1b[:, :], ones1_i)])
    chain([("vector", lambda v: v.memset(onesqb[:, :], 1.0))])

    def ln_stats(xt, A, B, C, sqb, xb, off=0):
        steps = []
        for i in range(NDT):
            steps.append(("scalar", lambda s, i=i: s.activation(
                xb[i][:, off:off + 2048], xt[i][:, 0:2048], AF.Identity)))
        chain(steps)

        def pe_sum(t):
            r = None
            for i in range(NDT):
                for c in range(0, 2048, 512):
                    r = t.matmul(P8[:, c:c + 512], onesqb[:, :],
                                 xb[i][:, off + c:off + c + 512],
                                 start=(i == 0), stop=(i == NDT - 1))
            return r
        chain([("tensor", pe_sum),
               ("vector", lambda v: v.tensor_scalar_mul(
                   A[:, 0:2048], P8[:, 0:2048], 1.0 / D))])
        for i in range(NDT):
            chain([("scalar", lambda s, i=i: s.activation(
                sqb[:, 0:2048], xb[i][:, off:off + 2048], AF.Square))])

            def pe_sq(t, i=i):
                r = None
                for c in range(0, 2048, 512):
                    r = t.matmul(P8[:, 2048 + c:2048 + c + 512], onesqb[:, :],
                                 sqb[:, c:c + 512], start=(i == 0),
                                 stop=(i == NDT - 1))
                return r
            chain([("tensor", pe_sq)])
        hs = slice(0, 2048)
        chain([
            ("vector", lambda v: v.tensor_scalar_mul(B[:, hs], P8[:, 2048:4096], 1.0 / D)),
            ("vector", lambda v: v.tensor_mul(C[:, hs], A[:, hs], A[:, hs])),
            ("vector", lambda v: v.tensor_sub(P8[:, hs], B[:, hs], C[:, hs])),
            ("vector", lambda v: v.tensor_scalar_add(B[:, hs], P8[:, hs], EPS)),
            ("vector", lambda v: v.reciprocal(C[:, hs], B[:, hs])),
            ("scalar", lambda s: s.activation(C[:, hs], C[:, hs], AF.Sqrt)),
        ])

    with ExitStack() as P:
        xbf = [sb(P, f"xbf{i}", [128, 2048], BF16) for i in range(NDT)]
        A = sb(P, "A", [128, 2048], F32)
        B = sb(P, "B", [128, 2048], F32)
        C = sb(P, "C", [128, 2048], F32)
        alf = sb(P, "alf", [N0, T], F32)
        dlf = sb(P, "dlf", [N0, T], F32)
        sqb = sb(P, "sqb", [128, 2048], BF16)
        wg = sb(P, "wg", [128, 4 * NDT * N0], F32)
        wgb = sb(P, "wgb", [128, 4 * NDT * N0], BF16)
        lnw = sb(P, "lnw", [128, NDT], F32)
        lnb = sb(P, "lnb", [128, NDT], F32)
        bpS = sb(P, "bpS", [128, NDT], F32)
        b4 = sb(P, "b4", [N0, 4], F32)
        wsum = sb(P, "wsum", [N0, 4], F32)
        wbsum = sb(P, "wbsum", [N0, 4], F32)
        wpS = sb(P, "wpS", [N0, D], F32)
        wpB = sb(P, "wpB", [N0, D], BF16)
        gsig = [sb(P, f"gsig{j}", [N0, T], F32) for j in range(4)]
        scm = sb(P, "scm", [N0, T + 16], F32)
        hseq = sb(P, "hseq", [N0, T], BF16)

        gdma([(wg[:, :], wg_i), (lnw[:, :], lnw_i), (lnb[:, :], lnb_i),
              (bpS[:, :], bp_i), (b4[:, :], b4_i), (wpS[:, :], wp_i)])
        steps = []
        for j in range(4):
            for i in range(NDT):
                sl = slice((j * NDT + i) * N0, (j * NDT + i + 1) * N0)
                steps.append(("vector", lambda v, sl=sl, i=i: v.tensor_scalar_mul(
                    wgb[:, sl], wg[:, sl], lnw[:, i:i + 1])))
        chain(steps)
        onecolb = sb(P, "onecolb", [128, 1], BF16)
        chain([("vector", lambda v: v.memset(onecolb[:, :], 1.0))])
        for j in range(4):
            chain([
                ("tensor", lambda t, j=j: mm(t, P8[0:N0, j * 512:j * 512 + 1],
                    [(wgb[:, (j * NDT + i) * N0:(j * NDT + i + 1) * N0],
                      onecolb[:, :]) for i in range(NDT)])),
                ("tensor", lambda t, j=j: mm(t, P8[0:N0, 2048 + j * 512:2048 + j * 512 + 1],
                    [(wg[:, (j * NDT + i) * N0:(j * NDT + i + 1) * N0],
                      lnb[:, i:i + 1]) for i in range(NDT)])),
            ])
        steps = []
        for j in range(4):
            steps.append(("vector", lambda v, j=j: v.tensor_copy(
                wsum[:, j:j + 1], P8[0:N0, j * 512:j * 512 + 1])))
            steps.append(("vector", lambda v, j=j: v.tensor_add(
                wbsum[:, j:j + 1], P8[0:N0, 2048 + j * 512:2048 + j * 512 + 1],
                b4[:, j:j + 1])))
        chain(steps)
        hs = slice(0, 2048)
        for half in range(2):
            lo = half * 2048
            gdma([(xbf[i][:, :], xin[i, :, lo:lo + 2048]) for i in range(NDT)])

            def pe_sum(t):
                r = None
                for i in range(NDT):
                    for c in range(0, 2048, 512):
                        r = t.matmul(P8[:, c:c + 512], onesqb[:, :],
                                     xbf[i][:, c:c + 512],
                                     start=(i == 0), stop=(i == NDT - 1))
                return r
            chain([("tensor", pe_sum),
                   ("vector", lambda v: v.tensor_scalar_mul(
                       A[:, hs], P8[:, hs], 1.0 / D))])
            for i in range(NDT):
                chain([
                    ("scalar", lambda s, i=i: s.activation(
                        sqb[:, :], xbf[i][:, :], AF.Square)),
                ])

                def pe_sq(t, i=i):
                    r = None
                    for c in range(0, 2048, 512):
                        r = t.matmul(P8[:, 2048 + c:2048 + c + 512], onesqb[:, :],
                                     sqb[:, c:c + 512], start=(i == 0),
                                     stop=(i == NDT - 1))
                    return r
                chain([("tensor", pe_sq)])
            chain([("vector", lambda v: v.tensor_scalar_mul(
                B[:, hs], P8[:, 2048:4096], 1.0 / D))])
            chain([
                ("vector", lambda v: v.tensor_mul(C[:, hs], A[:, hs], A[:, hs])),
                ("vector", lambda v: v.tensor_sub(P8[:, hs], B[:, hs], C[:, hs])),
                ("vector", lambda v: v.tensor_scalar_add(B[:, hs], P8[:, hs], EPS)),
                ("vector", lambda v: v.reciprocal(C[:, hs], B[:, hs])),
            ("scalar", lambda s: s.activation(C[:, hs], C[:, hs], AF.Sqrt)),
                ("vector", lambda v: v.tensor_mul(B[:, hs], P8[:, hs], C[:, hs])),
                ("vector", lambda v: v.tensor_mul(P8[:, hs], B[:, hs], C[:, hs])),
                ("vector", lambda v: v.tensor_scalar_add(B[:, hs], P8[:, hs], EPS)),
                ("vector", lambda v: v.reciprocal(P8[:, hs], B[:, hs])),
                ("scalar", lambda s: s.activation(B[:, hs], P8[:, hs], AF.Sqrt)),
                ("vector", lambda v: v.tensor_mul(P8[:, hs], C[:, hs], B[:, hs])),
                ("vector", lambda v, lo=lo: v.tensor_copy(
                    alf[:, lo:lo + 2048], P8[0:N0, hs])),
                ("vector", lambda v: v.tensor_mul(B[:, hs], A[:, hs], P8[:, hs])),
                ("vector", lambda v, lo=lo: v.tensor_scalar_mul(
                    dlf[:, lo:lo + 2048], B[0:N0, hs], -1.0)),
            ])
            for j in range(4):
                def pe_g(t, j=j):
                    r = None
                    for i in range(NDT):
                        sl = slice((j * NDT + i) * N0, (j * NDT + i + 1) * N0)
                        for c in range(0, 2048, 512):
                            r = t.matmul(P8[0:N0, c:c + 512], wgb[:, sl],
                                         xbf[i][:, c:c + 512], start=(i == 0),
                                         stop=(i == NDT - 1))
                    return r
                gl = slice(lo, lo + 2048)
                chain([
                    ("tensor", pe_g),
                    ("vector", lambda v, j=j, gl=gl: v.tensor_mul(
                        gsig[j][:, gl], P8[0:N0, hs], alf[:, gl])),
                    ("vector", lambda v, j=j, gl=gl: v.scalar_tensor_tensor(
                        P8[0:N0, 2048:4096], dlf[:, gl], wsum[:, j:j + 1],
                        gsig[j][:, gl], ALU.mult, ALU.add)),
                    ("vector", lambda v, j=j, gl=gl: v.tensor_scalar_add(
                        gsig[j][:, gl], P8[0:N0, 2048:4096], wbsum[:, j:j + 1])),
                ])
        chain([
            ("vector", lambda v: v.tensor_scalar(
                P8[0:N0, 0:T], gsig[0][:, :], 20.0, -20.0, ALU.min, ALU.max)),
            ("vector", lambda v: v.tensor_copy(gsig[0][:, :], P8[0:N0, 0:T])),
            ("vector", lambda v: v.tensor_scalar(
                P8[0:N0, 0:T], gsig[1][:, :], 20.0, -20.0, ALU.min, ALU.max)),
            ("vector", lambda v: v.tensor_copy(gsig[1][:, :], P8[0:N0, 0:T])),
            ("scalar", lambda s: s.activation(gsig[2][:, :], gsig[2][:, :], AF.Tanh)),
            ("scalar", lambda s: s.activation(gsig[3][:, :], gsig[3][:, :], AF.Sigmoid)),
            ("vector", lambda v: v.memset(scm[:, 0:1], 0.0)),
            ("vector", lambda v: v.tensor_tensor_scan(
                scm[:, 1:T + 1], gsig[1][:, :], gsig[0][:, :], 0.0,
                ALU.add, ALU.max)),
            ("vector", lambda v: v.tensor_add(P8[0:N0, 0:T], gsig[1][:, :], scm[:, 0:T])),
            ("vector", lambda v: v.tensor_sub(gsig[1][:, :], P8[0:N0, 0:T], scm[:, 1:T + 1])),
            ("scalar", lambda s: s.activation(gsig[1][:, :], gsig[1][:, :], AF.Exp)),
            ("vector", lambda v: v.tensor_sub(P8[0:N0, 0:T], gsig[0][:, :], scm[:, 1:T + 1])),
            ("scalar", lambda s: s.activation(gsig[0][:, :], P8[0:N0, 0:T], AF.Exp)),
            ("vector", lambda v: v.tensor_mul(scm[:, 0:T], gsig[0][:, :], gsig[2][:, :])),
            ("vector", lambda v: v.tensor_tensor_scan(
                P8[0:N0, 0:T], gsig[1][:, :], scm[:, 0:T], 0.0, ALU.mult, ALU.add)),
            ("scalar", lambda s: s.activation(gsig[0][:, :], P8[0:N0, 0:T], AF.Tanh)),
            ("vector", lambda v: v.tensor_mul(hseq[:, :], gsig[3][:, :], gsig[0][:, :])),
            ("vector", lambda v: v.tensor_copy(wpB[:, :], wpS[:, :])),
        ])
        for half in range(2):
            lo = half * 2048
            for i in range(NDT):
                def pe_o(t, i=i, lo=lo):
                    r = None
                    for c in range(0, 2048, 512):
                        r = t.matmul(P8[:, c:c + 512], wpB[:, i * 128:(i + 1) * 128],
                                     hseq[:, lo + c:lo + c + 512], start=True,
                                     stop=True)
                    return r
                chain([
                    ("tensor", pe_o),
                    ("vector", lambda v, i=i: v.tensor_scalar_add(
                        B[:, :], P8[:, 0:2048], bpS[:, i:i + 1])),
                ])
                gdma([(A[:, :], xin[i, :, lo:lo + 2048])])
                chain([("vector", lambda v: v.tensor_add(C[:, :], B[:, :], A[:, :]))])
                pre = {s: cg(s) for s in TICKS}

                @block.gpsimd
                def _(g, pre=pre, i=i, lo=lo):
                    for s, v in pre.items():
                        if v:
                            g.wait_ge(sems[s], v)
                    tick(g.dma_start(out=xd1[i, :, lo:lo + 2048], in_=C[:, :]),
                         "gd", 16)
        join_all()

    def gstore(dst, src):
        pre = {s: cg(s) for s in TICKS}

        @block.gpsimd
        def _(g):
            for s, v in pre.items():
                if v:
                    g.wait_ge(sems[s], v)
            tick(g.dma_start(out=dst, in_=src), "gd", 16)

    with ExitStack() as P:
        xa = [sb(P, f"xa{i}", [128, 2048], F32) for i in range(NDT)]
        xnb = [sb(P, f"xnb{i}", [128, 2048], BF16) for i in range(NDT)]
        sqb = sb(P, "sqb0", [128, 2048], BF16)
        mu = sb(P, "mu0", [128, 2048], F32)
        r1 = sb(P, "r10", [128, 2048], F32)
        sc = sb(P, "sc0", [128, 2048], F32)
        gw0 = sb(P, "gw0", [128, NDT * E0], BF16)
        scr = sb(P, "scr0", [E0, 2048], F32)
        stm = sb(P, "stm0", [128, 16 * E0], F32)
        gtm = sb(P, "gtm0", [128, 16 * E0], F32)
        t1 = sb(P, "t10", [128, E0], F32)
        t2 = sb(P, "t20", [128, E0], F32)
        t3 = sb(P, "t30", [128, E0], F32)
        t4 = sb(P, "t40", [128, E0], F32)
        m1 = sb(P, "m10", [128, 1], F32)
        m2 = sb(P, "m20", [128, 1], F32)
        p1 = sb(P, "p10", [128, 1], F32)
        p2 = sb(P, "p20", [128, 1], F32)
        gge = sb(P, "gge0", [E0, 2048], F32)
        ggeb = sb(P, "ggeb0", [E0, 2048], BF16)
        w12s = [sb(P, f"w12s{i}", [128, 2048], BF16) for i in range(NDT)]
        w3s = [sb(P, f"w3s{k}", [128, D], BF16) for k in range(8)]
        hid = [sb(P, f"hid{j}", [128, 512], BF16) for j in range(8)]
        silA = sb(P, "silA", [128, 512], F32)
        gbcf = sb(P, "gbcf", [128, 512], BF16)
        g1b = sb(P, "g1b", [1, 512], BF16)

        gdma([(gw0[:, :], gw0_i)])
        for half in range(2):
            lo = half * 2048
            gdma([(xa[i][:, :], xd1[i, :, lo:lo + 2048]) for i in range(NDT)])
            ln_stats(xa, mu, sc, r1, sqb, xnb)
            for i in range(NDT):
                chain([
                    ("vector", lambda v, i=i: v.tensor_sub(sc[:, :], xa[i][:, :], mu[:, :])),
                    ("vector", lambda v, i=i: v.tensor_mul(xnb[i][:, :], sc[:, :], r1[:, :])),
                ])
            def pe_s(t):
                r = None
                for i in range(NDT):
                    for c in range(0, 2048, 512):
                        r = t.matmul(P8[0:E0, c:c + 512], gw0[:, i * E0:(i + 1) * E0],
                                     xnb[i][:, c:c + 512], start=(i == 0),
                                     stop=(i == NDT - 1))
                return r
            chain([("tensor", pe_s),
                   ("vector", lambda v: v.tensor_copy(scr[:, :], P8[0:E0, 0:2048]))])
            def pe_t(t):
                r = None
                for g in range(16):
                    r = t.transpose(P8[:, 2048 + g * E0:2048 + (g + 1) * E0],
                                    scr[:, g * 128:(g + 1) * 128],
                                    identf[0:E0, 0:E0])
                return r
            chain([("tensor", pe_t),
                   ("vector", lambda v: v.tensor_copy(stm[:, :], P8[:, 2048:2048 + 128]))])
            for g in range(16):
                gs = slice(g * E0, (g + 1) * E0)
                chain([
                    ("vector", lambda v, gs=gs: v.reduce_max(m1[:, :], stm[:, gs], AX)),
                    ("vector", lambda v, gs=gs: v.tensor_scalar(
                        t1[:, :], stm[:, gs], m1[:, 0:1], None, ALU.is_equal)),
                    ("vector", lambda v, gs=gs: v.scalar_tensor_tensor(
                        t2[:, :], t1[:, :], -1e9, stm[:, gs], ALU.mult, ALU.add)),
                    ("vector", lambda v: v.reduce_max(m2[:, :], t2[:, :], AX)),
                    ("vector", lambda v, gs=gs: v.tensor_scalar(
                        t3[:, :], t2[:, :], m2[:, 0:1], None, ALU.is_equal)),
                    ("vector", lambda v: v.tensor_sub(p1[:, :], m1[:, :], m2[:, :])),
                    ("scalar", lambda s: s.activation(p1[:, :], p1[:, :], AF.Sigmoid)),
                    ("vector", lambda v: v.tensor_scalar(
                        p2[:, :], p1[:, :], -1.0, 1.0, ALU.mult, ALU.add)),
                    ("vector", lambda v, gs=gs: v.tensor_scalar_mul(
                        t4[:, :], t1[:, :], p1[:, 0:1])),
                    ("vector", lambda v, gs=gs: v.tensor_scalar_mul(
                        t2[:, :], t3[:, :], p2[:, 0:1])),
                    ("vector", lambda v, gs=gs: v.tensor_add(
                        gtm[:, gs], t4[:, :], t2[:, :])),
                ])
            def pe_b(t):
                r = None
                for g in range(16):
                    r = t.transpose(P8[0:E0, g * 128:(g + 1) * 128],
                                    gtm[:, g * E0:(g + 1) * E0], identf[:, :])
                return r
            chain([("tensor", pe_b),
                   ("vector", lambda v: v.tensor_copy(gge[:, :], P8[0:E0, 0:2048])),
                   ("vector", lambda v: v.tensor_copy(ggeb[:, :], gge[:, :]))])
            for e in range(E0):
                for grp in range(2):
                    gdma([(w12s[i][:, 0:1024],
                           w12_i[e, i, :, grp * 1024:(grp + 1) * 1024])
                          for i in range(NDT)] +
                         [(w12s[i][:, 1024:2048],
                           w12_i[e, i, :, 2048 + grp * 1024:2048 + (grp + 1) * 1024])
                          for i in range(NDT)] +
                         [(w3s[k][:, :], w3_i[e, grp * 8 + k, :, :])
                          for k in range(8)])
                    for ch in range(4):
                        cs = slice(ch * 512, (ch + 1) * 512)
                        gdma([(g1b[0:1, :], ggeb[e:e + 1, cs])])
                        chain([("tensor", lambda t, cs=cs: mm(
                            t, P8[:, 1024:1536],
                            [(ones1b[:, :], g1b[0:1, 0:512])])),
                            ("vector", lambda v: v.tensor_copy(
                                gbcf[:, :], P8[:, 1024:1536]))])
                        for j in range(8):
                            def pe_h(t, j=j, cs=cs):
                                r = None
                                for i in range(NDT):
                                    r = t.matmul(P8[:, 0:512],
                                                 w12s[i][:, j * 128:(j + 1) * 128],
                                                 xnb[i][:, cs], start=(i == 0),
                                                 stop=(i == NDT - 1))
                                for i in range(NDT):
                                    r = t.matmul(P8[:, 512:1024],
                                                 w12s[i][:, 1024 + j * 128:1024 + (j + 1) * 128],
                                                 xnb[i][:, cs], start=(i == 0),
                                                 stop=(i == NDT - 1))
                                return r
                            chain([
                                ("tensor", pe_h),
                                ("scalar", lambda s: s.activation(
                                    silA[:, :], P8[:, 0:512], AF.Silu)),
                                ("vector", lambda v: v.tensor_mul(
                                    sc[:, 0:512], silA[:, :], P8[:, 512:1024])),
                                ("vector", lambda v, j=j: v.tensor_mul(
                                    hid[j][:, :], sc[:, 0:512], gbcf[:, :])),
                            ])
                        for d in range(NDT):
                            chain([
                                ("tensor", lambda t, d=d: mm(
                                    t, P8[:, 1536:2048],
                                    [(w3s[k][:, d * 128:(d + 1) * 128], hid[k][:, :])
                                     for k in range(8)])),
                                ("vector", lambda v, d=d, cs=cs: v.tensor_add(
                                    sc[:, 0:512], xa[d][:, cs], P8[:, 1536:2048])),
                                ("vector", lambda v, d=d, cs=cs: v.tensor_copy(
                                    xa[d][:, cs], sc[:, 0:512])),
                            ])
            for i in range(NDT):
                gstore(xd2[i, :, lo:lo + 2048], xa[i][:, :])
        join_all()

    with ExitStack() as P:
        xn1 = [sb(P, f"xn1_{i}", [128, T], BF16) for i in range(NDT)]
        xa = [sb(P, f"xaa{i}", [128, 2048], F32) for i in range(NDT)]
        sqb = sb(P, "sqb1", [128, 2048], BF16)
        mu = sb(P, "mu1", [128, 2048], F32)
        r1 = sb(P, "r11", [128, 2048], F32)
        sc = sb(P, "sc1", [128, 2048], F32)
        qkvw = sb(P, "qkvw", [128, NDT * 3 * DH], BF16)
        whS = sb(P, "whS", [DH, D], BF16)
        qfm = sb(P, "qfm", [DH, T], BF16)
        kfm = sb(P, "kfm", [DH, T], BF16)
        vfm = sb(P, "vfm", [DH, T], BF16)
        vtm = sb(P, "vtm", [128, 32 * DH], BF16)
        Pb = sb(P, "Pb", [128, 2048], BF16)
        cmask = sb(P, "cmask", [128, 4 * 512], BF16)
        yhb = sb(P, "yhb", [DH, 512], BF16)
        sc5 = sb(P, "sc5", [128, 512], F32)
        sc6 = sb(P, "sc6", [128, 512], F32)
        sums_sb = sb(P, "sums_sb", [128, 512], F32)
        y_sb = sb(P, "y_sb", [DH, 512], F32)

        gdma([(cmask[:, q * 512:(q + 1) * 512], cmask_i[q, :, :]) for q in range(4)])
        for half in range(2):
            lo = half * 2048
            gdma([(xa[i][:, :], xd2[i, :, lo:lo + 2048]) for i in range(NDT)])
            ln_stats(xa, mu, sc, r1, sqb, xn1, off=lo)
            for i in range(NDT):
                chain([
                    ("vector", lambda v, i=i: v.tensor_sub(sc[:, :], xa[i][:, :], mu[:, :])),
                    ("vector", lambda v, i=i, lo=lo: v.tensor_mul(
                        xn1[i][:, lo:lo + 2048], sc[:, :], r1[:, :])),
                ])
        for half in range(2):
            lo = half * 2048
            gdma([(xa[i][:, :], xd2[i, :, lo:lo + 2048]) for i in range(NDT)])
            for h in range(NH):
                gdma([(qkvw[:, i * 3 * DH:i * 3 * DH + DH], q_i[h, i, :, :])
                      for i in range(NDT)] +
                     [(qkvw[:, i * 3 * DH + DH:i * 3 * DH + 2 * DH], k_i[h, i, :, :])
                      for i in range(NDT)] +
                     [(qkvw[:, i * 3 * DH + 2 * DH:i * 3 * DH + 3 * DH], v_i[h, i, :, :])
                      for i in range(NDT)] + [(whS[:, :], wh_i[h, :, :])])
                for (dst, off) in ((qfm, 0), (kfm, DH), (vfm, 2 * DH)):
                    def pe_q(t, dst=dst, off=off):
                        r = None
                        for c in range(0, T, 512):
                            for i in range(NDT):
                                r = t.matmul(P8[0:DH, c:c + 512],
                                             qkvw[:, i * 3 * DH + off:i * 3 * DH + off + DH],
                                             xn1[i][:, c:c + 512], start=(i == 0),
                                             stop=(i == NDT - 1))
                        return r
                    chain([("tensor", pe_q),
                           ("vector", lambda v, dst=dst: v.tensor_scalar_mul(
                               dst[:, :], P8[0:DH, :], 0.25))])
                for c in range(32):
                    chain([
                        ("tensor", lambda t, c=c: t.matmul(
                            P8[:, 0:DH], vfm[:, c * 128:(c + 1) * 128],
                            identb[0:DH, 0:DH], start=True, stop=True)),
                        ("vector", lambda v, c=c: v.tensor_copy(
                            vtm[:, c * DH:(c + 1) * DH], P8[:, 0:DH])),
                    ])
                for ct in range(4):
                    cs0 = lo + ct * 512
                    ncc = (cs0 + 512) // 128
                    chain([
                        ("vector", lambda v: v.memset(sums_sb[:, :], 0.0)),
                        ("vector", lambda v: v.memset(y_sb[:, :], 0.0)),
                    ])
                    for j in range(ncc):
                        def pe_st(t, j=j, cs0=cs0):
                            return t.matmul(P8[:, 0:512], kfm[:, j * 128:(j + 1) * 128],
                                            qfm[:, cs0:cs0 + 512], start=True, stop=True)
                        st = [("tensor", pe_st),
                              ("vector", lambda v: v.tensor_scalar_mul(
                                  sc5[:, :], P8[:, 0:512], 1.0 / 64.0))]
                        if j * 128 >= cs0:
                            qq = (j * 128 - cs0) // 128
                            st.append(("vector", lambda v, qq=qq: v.tensor_add(
                                sc6[:, :], sc5[:, :], cmask[:, qq * 512:(qq + 1) * 512])))
                            st.append(("scalar", lambda s, j=j: s.activation(
                                Pb[:, (j % 4) * 512:(j % 4) * 512 + 512],
                                sc6[:, :], AF.Exp)))
                        else:
                            st.append(("scalar", lambda s, j=j: s.activation(
                                Pb[:, (j % 4) * 512:(j % 4) * 512 + 512],
                                sc5[:, :], AF.Exp)))
                        chain(st)
                        if (j % 4) == 3 or j == ncc - 1:
                            npan = (j % 4) + 1
                            base = j - npan + 1

                            def pe_sum(t, npan=npan):
                                r = None
                                for p_ in range(npan):
                                    r = t.matmul(P8[:, 1024:1536], onesqb[:, :],
                                                 Pb[:, p_ * 512:p_ * 512 + 512],
                                                 start=(p_ == 0), stop=(p_ == npan - 1))
                                return r

                            def pe_yy(t, base=base, npan=npan):
                                r = None
                                for p_ in range(npan):
                                    cchunk = base + p_
                                    r = t.matmul(P8[0:DH, 1536:2048],
                                                 vtm[:, cchunk * DH:(cchunk + 1) * DH],
                                                 Pb[:, p_ * 512:p_ * 512 + 512],
                                                 start=(p_ == 0), stop=(p_ == npan - 1))
                                return r
                            chain([
                                ("tensor", pe_sum),
                                ("vector", lambda v: v.tensor_add(
                                    sc5[:, :], sums_sb[:, :], P8[:, 1024:1536])),
                                ("vector", lambda v: v.tensor_copy(
                                    sums_sb[:, :], sc5[:, :])),
                                ("tensor", pe_yy),
                                ("vector", lambda v: v.tensor_add(
                                    sc5[0:DH, :], y_sb[:, :], P8[0:DH, 1536:2048])),
                                ("vector", lambda v: v.tensor_copy(
                                    y_sb[:, :], sc5[0:DH, :])),
                            ])
                    chain([
                        ("vector", lambda v: v.reciprocal(sc6[:, :], sums_sb[:, :])),
                        ("vector", lambda v: v.tensor_mul(
                            yhb[:, :], y_sb[:, :], sc6[0:DH, :])),
                    ])
                    for d in range(NDT):
                        chain([
                            ("tensor", lambda t, d=d: t.matmul(
                                P8[:, 2048:2560], whS[:, d * 128:(d + 1) * 128],
                                yhb[:, :], start=True, stop=True)),
                            ("vector", lambda v, d=d, ct=ct: v.scalar_tensor_tensor(
                                sc5[:, :], P8[:, 2048:2560], 0.25,
                                xa[d][:, ct * 512:(ct + 1) * 512], ALU.mult, ALU.add)),
                            ("vector", lambda v, d=d, ct=ct: v.tensor_copy(
                                xa[d][:, ct * 512:(ct + 1) * 512], sc5[:, :])),
                        ])
            for i in range(NDT):
                gstore(xd1[i, :, lo:lo + 2048], xa[i][:, :])
        join_all()

    with ExitStack() as P:
        xn2 = [sb(P, f"xn2_{i}", [128, T], BF16) for i in range(NDT)]
        gw1 = sb(P, "gw1", [128, NDT * E1], BF16)
        biasbc = sb(P, "biasbc", [128, E1], F32)
        geb = sb(P, "geb1", [E1, T], BF16)

        gdma([(gw1[:, :], gw1_i), (biasbc[:, :], biasbc_i)])
        with ExitStack() as PA:
            xa = [sb(PA, f"xab{i}", [128, 2048], F32) for i in range(NDT)]
            sqb = sb(PA, "sqb2", [128, 2048], BF16)
            mu = sb(PA, "mu2", [128, 2048], F32)
            r1 = sb(PA, "r12", [128, 2048], F32)
            sc = sb(PA, "sc2", [128, 2048], F32)
            for half in range(2):
                lo = half * 2048
                gdma([(xa[i][:, :], xd1[i, :, lo:lo + 2048]) for i in range(NDT)])
                ln_stats(xa, mu, sc, r1, sqb, xn2, off=lo)
                for i in range(NDT):
                    chain([
                        ("vector", lambda v, i=i: v.tensor_sub(
                            sc[:, :], xa[i][:, :], mu[:, :])),
                        ("vector", lambda v, i=i, lo=lo: v.tensor_mul(
                            xn2[i][:, lo:lo + 2048], sc[:, :], r1[:, :])),
                    ])
        with ExitStack() as PG:
            scr = sb(PG, "scr1", [E1, T], F32)
            self_sel = sb(PG, "selfm", [E1, T], BF16)
            gfm = sb(PG, "gfm", [E1, T], BF16)
            stm = sb(PG, "stm1", [128, E1], F32)
            cur = sb(PG, "cur1", [128, E1], F32)
            cur2 = sb(PG, "cur2", [128, E1], F32)
            selt = sb(PG, "selt", [128, E1], F32)
            selk = sb(PG, "selk", [128, E1], F32)
            wselt = sb(PG, "wselt", [128, E1], F32)
            mk = sb(PG, "mk1", [128, 1], F32)
            ws = sb(PG, "ws1", [128, 1], F32)
            cnts = sb(PG, "cnts", [8, E1], F32)
            basem = sb(PG, "basem", [E1, 8], F32)
            bmt = sb(PG, "bmt", [E1, 8], F32)
            sk4 = [sb(PG, f"sk4_{k}", [128, E1], F32) for k in range(4)]
            base = sb(PG, "base", [E1, 1], F32)
            gdma([(basem[:, :], basemask_i)])
            for c in range(0, T, 512):
                def pe_s1(t, c=c):
                    r = None
                    for i in range(NDT):
                        r = t.matmul(P8[0:E1, 0:512], gw1[:, i * E1:(i + 1) * E1],
                                     xn2[i][:, c:c + 512], start=(i == 0),
                                     stop=(i == NDT - 1))
                    return r
                chain([("tensor", pe_s1),
                       ("scalar", lambda s, c=c: s.activation(
                           scr[:, c:c + 512], P8[0:E1, 0:512], AF.Sigmoid))])
            for g in range(32):
                chain([
                    ("tensor", lambda t, g=g: t.transpose(
                        P8[:, 0:E1], scr[:, g * 128:(g + 1) * 128],
                        identf[0:E1, 0:E1])),
                    ("vector", lambda v: v.tensor_copy(stm[:, :], P8[:, 0:E1])),
                    ("vector", lambda v: v.tensor_add(cur[:, :], stm[:, :], biasbc[:, :])),
                    ("vector", lambda v: v.memset(selt[:, :], 0.0)),
                ])
                for k in range(4):
                    a, b = (cur, cur2) if k % 2 == 0 else (cur2, cur)
                    chain([
                        ("vector", lambda v, a=a: v.reduce_max(mk[:, :], a[:, :], AX)),
                        ("vector", lambda v, a=a, k=k: v.tensor_scalar(
                            sk4[k][:, :], a[:, :], mk[:, 0:1], None, ALU.is_equal)),
                        ("vector", lambda v, a=a, b=b, k=k: v.scalar_tensor_tensor(
                            b[:, :], sk4[k][:, :], -1e9, a[:, :], ALU.mult, ALU.add)),
                    ])
                chain([
                    ("vector", lambda v: v.tensor_add(selk[:, :], sk4[0][:, :], sk4[1][:, :])),
                    ("vector", lambda v: v.tensor_add(wselt[:, :], sk4[2][:, :], sk4[3][:, :])),
                    ("vector", lambda v: v.tensor_add(selt[:, :], selk[:, :], wselt[:, :])),
                    ("vector", lambda v: v.tensor_mul(wselt[:, :], selt[:, :], stm[:, :])),
                    ("vector", lambda v: v.reduce_sum(mk[:, :], wselt[:, :], AX)),
                    ("vector", lambda v: v.tensor_scalar_max(ws[:, :], mk[:, :], 1e-12)),
                    ("vector", lambda v: v.reciprocal(mk[:, :], ws[:, :])),
                    ("vector", lambda v: v.tensor_scalar_mul(
                        selk[:, :], wselt[:, :], mk[:, 0:1])),
                    ("vector", lambda v: v.tensor_copy(wselt[:, :], selk[:, :])),
                    ("tensor", lambda t, g=g: t.transpose(
                        P8[0:E1, 512:512 + 128], wselt[:, :], identf[:, :])),
                    ("vector", lambda v, g=g: v.tensor_copy(
                        gfm[:, g * 128:(g + 1) * 128], P8[0:E1, 512:512 + 128])),
                    ("tensor", lambda t, g=g: t.transpose(
                        P8[0:E1, 1024:1024 + 128], selt[:, :], identf[:, :])),
                    ("vector", lambda v, g=g: v.tensor_copy(
                        self_sel[:, g * 128:(g + 1) * 128], P8[0:E1, 1024:1024 + 128])),
                ])
            incl = scr
            chain([
                ("vector", lambda v: v.memset(P8[0:E1, 0:T], 0.0)),
                ("vector", lambda v: v.tensor_tensor_scan(
                    incl[:, :], P8[0:E1, 0:T], self_sel[:, :], 0.0, ALU.add, ALU.add)),
            ])
            gstore(cc_in.ap()[:, :], incl[:, T - 1:T])
            coll()
            gdma([(cnts[:, :], cc_out.ap().rearrange("a b c -> a (b c)"))])
            chain([
                ("tensor", lambda t: t.transpose(P8[0:E1, 0:8], cnts[:, :],
                                                 identf[0:8, 0:8])),
                ("vector", lambda v: v.tensor_mul(bmt[:, :], basem[:, :], P8[0:E1, 0:8])),
                ("vector", lambda v: v.reduce_sum(base[:, :], bmt[:, :], AX)),
                ("vector", lambda v: v.tensor_sub(P8[0:E1, 0:T], incl[:, :], self_sel[:, :])),
                ("vector", lambda v: v.tensor_scalar_add(
                    incl[:, :], P8[0:E1, 0:T], base[:, 0:1])),
                ("vector", lambda v: v.tensor_scalar(
                    P8[0:E1, 0:T], incl[:, :], float(C1), None, ALU.is_lt)),
                ("vector", lambda v: v.tensor_mul(geb[:, :], gfm[:, :], P8[0:E1, 0:T])),
            ])
        with ExitStack() as PC:
            xa = [sb(PC, f"xac{i}", [128, 2048], F32) for i in range(NDT)]
            sc = sb(PC, "sc2c", [128, 2048], F32)
            w1s = [sb(PC, f"w1s{i}", [128, I1], BF16) for i in range(NDT)]
            w3s2 = [sb(PC, f"w3s2{i}", [128, I1], BF16) for i in range(NDT)]
            w2s = [sb(PC, f"w2s{k}", [128, D], BF16) for k in range(4)]
            hid = [sb(PC, f"hid2_{j}", [128, 512], BF16) for j in range(4)]
            silA = sb(PC, "silA2", [128, 512], F32)
            gbcf = sb(PC, "gbcf2", [128, 512], BF16)
            g1c = sb(PC, "g1c", [1, 512], BF16)
            for half in range(2):
                lo = half * 2048
                gdma([(xa[i][:, :], xd1[i, :, lo:lo + 2048])
                      for i in range(NDT)])
                for e in range(E1 + 1):
                    shared = (e == E1)
                    if shared:
                        gdma([(w1s[i][:, :], sw1_i[i, :, :]) for i in range(NDT)] +
                             [(w3s2[i][:, :], sw3_i[i, :, :]) for i in range(NDT)] +
                             [(w2s[k][:, :], sw2_i[k, :, :]) for k in range(4)])
                    else:
                        gdma([(w1s[i][:, :], w1_i[e, i, :, :]) for i in range(NDT)] +
                             [(w3s2[i][:, :], w3e_i[e, i, :, :]) for i in range(NDT)] +
                             [(w2s[k][:, :], w2_i[e, k, :, :]) for k in range(4)])
                    for ch in range(4):
                        cs = slice(ch * 512, (ch + 1) * 512)
                        gcs = slice(lo + ch * 512, lo + (ch + 1) * 512)
                        if not shared:
                            gdma([(g1c[0:1, 0:512], geb[e:e + 1, gcs])])
                            chain([("tensor", lambda t: mm(
                                t, P8[:, 1024:1536],
                                [(ones1b[:, :], g1c[0:1, 0:512])])),
                                ("vector", lambda v: v.tensor_copy(
                                    gbcf[:, :], P8[:, 1024:1536]))])
                        for j in range(4):
                            def pe_h(t, j=j, gcs=gcs):
                                r = None
                                for i in range(NDT):
                                    r = t.matmul(P8[:, 0:512],
                                                 w1s[i][:, j * 128:(j + 1) * 128],
                                                 xn2[i][:, gcs], start=(i == 0),
                                                 stop=(i == NDT - 1))
                                for i in range(NDT):
                                    r = t.matmul(P8[:, 512:1024],
                                                 w3s2[i][:, j * 128:(j + 1) * 128],
                                                 xn2[i][:, gcs], start=(i == 0),
                                                 stop=(i == NDT - 1))
                                return r
                            st = [
                                ("tensor", pe_h),
                                ("scalar", lambda s: s.activation(
                                    silA[:, :], P8[:, 0:512], AF.Silu)),
                                ("vector", lambda v: v.tensor_mul(
                                    sc[:, 0:512], silA[:, :], P8[:, 512:1024])),
                            ]
                            if shared:
                                st.append(("vector", lambda v, j=j: v.tensor_copy(
                                    hid[j][:, :], sc[:, 0:512])))
                            else:
                                st.append(("vector", lambda v, j=j: v.tensor_mul(
                                    hid[j][:, :], sc[:, 0:512], gbcf[:, :])))
                            chain(st)
                        for d in range(NDT):
                            chain([
                                ("tensor", lambda t, d=d: mm(
                                    t, P8[:, 1536:2048],
                                    [(w2s[k][:, d * 128:(d + 1) * 128], hid[k][:, :])
                                     for k in range(4)])),
                                ("vector", lambda v, d=d, cs=cs: v.tensor_add(
                                    sc[:, 0:512], xa[d][:, cs], P8[:, 1536:2048])),
                                ("vector", lambda v, d=d, cs=cs: v.tensor_copy(
                                    xa[d][:, cs], sc[:, 0:512])),
                            ])
                for i in range(NDT):
                    gstore(y[i, :, lo:lo + 2048], xa[i][:, :])
        join_all()

    es.close()
    return nc


def kernel(**inputs):
    import ml_dtypes
    f32 = np.float32
    bf = ml_dtypes.bfloat16
    x = np.asarray(inputs["L0_attn_x"], f32)

    def fm(a):
        return np.ascontiguousarray(a.T.reshape(NDT, 128, a.shape[0]))

    Wg = [np.asarray(inputs["L0_attn_W" + k], f32) for k in "ifzo"]
    wg = np.zeros((128, 4 * NDT * N0), f32)
    for j in range(4):
        for i in range(NDT):
            wg[:, (j * NDT + i) * N0:(j * NDT + i + 1) * N0] = \
                Wg[j][:, i * 128:(i + 1) * 128].T
    lnw = np.ascontiguousarray(
        np.asarray(inputs["L0_attn_ln_w"], f32).reshape(NDT, 128).T)
    lnb = np.ascontiguousarray(
        np.asarray(inputs["L0_attn_ln_b"], f32).reshape(NDT, 128).T)
    bp = np.ascontiguousarray(
        np.asarray(inputs["L0_attn_bp"], f32).reshape(NDT, 128).T)
    b4 = np.stack([np.asarray(inputs["L0_attn_b" + k], f32) for k in "ifzo"],
                  axis=1)
    wp = np.ascontiguousarray(np.asarray(inputs["L0_attn_Wp"], f32).T)
    w12 = np.asarray(inputs["L0_ffn_w12"], f32)
    w12t = np.ascontiguousarray(
        w12.transpose(0, 2, 1).reshape(E0, NDT, 128, 2 * H0))
    w3 = np.asarray(inputs["L0_ffn_w3"], f32)
    w3t = np.ascontiguousarray(w3.transpose(0, 2, 1).reshape(E0, 16, 128, D))
    gw0 = np.asarray(inputs["L0_ffn_gate_w"], f32)
    gw0t = np.zeros((128, NDT * E0), f32)
    for i in range(NDT):
        gw0t[:, i * E0:(i + 1) * E0] = gw0[:, i * 128:(i + 1) * 128].T
    Q = np.asarray(inputs["L1_attn_Q"], f32)
    K = np.asarray(inputs["L1_attn_K"], f32)
    V = np.asarray(inputs["L1_attn_V"], f32)
    W = np.asarray(inputs["L1_attn_W"], f32)
    qt = np.ascontiguousarray(Q.transpose(2, 1, 0).reshape(NH, NDT, 128, DH))
    kt = np.ascontiguousarray(K.transpose(2, 1, 0).reshape(NH, NDT, 128, DH))
    vt = np.ascontiguousarray(V.transpose(2, 1, 0).reshape(NH, NDT, 128, DH))
    wh = np.ascontiguousarray(W.transpose(2, 1, 0).reshape(NH, DH, D))
    gw1 = np.asarray(inputs["L1_ffn_gate_w"]).astype(bf)
    gw1t = np.zeros((128, NDT * E1), bf)
    for i in range(NDT):
        gw1t[:, i * E1:(i + 1) * E1] = gw1[:, i * 128:(i + 1) * 128].T
    biasbc = np.tile(np.asarray(inputs["L1_ffn_bias"], f32)[None, :], (128, 1))
    w1 = np.asarray(inputs["L1_ffn_W1"]).astype(bf).reshape(E1, NDT, 128, I1)
    w3e = np.asarray(inputs["L1_ffn_W3"]).astype(bf).reshape(E1, NDT, 128, I1)
    w2 = np.asarray(inputs["L1_ffn_W2"]).astype(bf).reshape(E1, 4, 128, D)
    sw1 = np.ascontiguousarray(
        np.asarray(inputs["L1_ffn_sw1"]).astype(bf).T).reshape(NDT, 128, I1)
    sw3 = np.ascontiguousarray(
        np.asarray(inputs["L1_ffn_sw3"]).astype(bf).T).reshape(NDT, 128, I1)
    sw2 = np.ascontiguousarray(
        np.asarray(inputs["L1_ffn_sw2"]).astype(bf).T).reshape(4, 128, D)
    identb = np.eye(128, dtype=bf)
    identf = np.eye(128, dtype=f32)
    cmask = np.zeros((4, 128, 512), f32)
    for q in range(4):
        r = np.arange(128)[:, None]
        c = np.arange(512)[None, :]
        cmask[q] = np.where(c >= q * 128 + r, 0.0, NEG)
    ones1 = np.ones((1, 128), bf)

    nc = build_program()
    in_maps = []
    for core in range(8):
        b = core // 2
        bm = np.zeros((E1, 8), f32)
        for cp in range(8):
            if cp % 2 == 0 and cp // 2 < b:
                bm[:, cp] = 1.0
        in_maps.append({
            "xin": fm(x[b]), "wg_i": wg, "lnw_i": lnw, "lnb_i": lnb,
            "bp_i": bp, "b4_i": b4, "wp_i": wp, "w12_i": w12t, "w3_i": w3t,
            "gw0_i": gw0t, "q_i": qt, "k_i": kt, "v_i": vt, "wh_i": wh,
            "gw1_i": gw1t, "biasbc_i": biasbc, "w1_i": w1, "w3e_i": w3e,
            "w2_i": w2, "sw1_i": sw1, "sw3_i": sw3, "sw2_i": sw2,
            "identb_i": identb, "identf_i": identf, "cmask_i": cmask,
            "basemask_i": bm, "ones1_i": ones1,
        })
    res = run_bass_kernel_spmd(nc, in_maps, core_ids=list(range(8)))
    outs = res.results
    full = np.zeros((4, T, D), f32)
    for b in range(4):
        r = outs[2 * b]
        ya = r["y"] if isinstance(r, dict) else r[0]
        full[b] = np.asarray(ya, f32).reshape(D, T).T
    return full
